# version 02
# baseline (speedup 1.0000x reference)
import os as _os
import tempfile as _tempfile
_os.environ["NEURON_COMPILE_CACHE_URL"] = _tempfile.mkdtemp(prefix="ncc_cache_")

import numpy as np
import ml_dtypes
from contextlib import ExitStack

import concourse.bass as bass
import concourse.mybir as mybir
from concourse.bass_utils import run_bass_kernel_spmd

F32 = mybir.dt.float32
BF16 = mybir.dt.bfloat16
AF = mybir.ActivationFunctionType
OP = mybir.AluOpType
AX = mybir.AxisListType

D, T, I = 1024, 2048, 3072
NH, HD = 16, 64
FC, TC, IC = D // 128, T // 512, I // 128
NCORE = 8

MAGIC = float(np.float32(1.5 * 2 ** 23))
QMAX = 32767.0
U16MAX = 65535.0
EPS_LN = 1e-5
EPS_RMS = 1e-6
ROPE_BASE = 10000.0
WQMAX = 7.0
WBLK = 32

SUBLAYERS = 3
DEBUG_TAPS = False
ATTN_PART = 3
ATTN_SUM = True


class Prog:
    ENGINES = ("sync", "tensor", "vector", "scalar", "gpsimd")

    def __init__(self, nc, es):
        self.nc = nc
        self.ops = {e: [] for e in self.ENGINES}
        self.sems = {}
        for s in ("pe", "dve", "act", "gps", "ld", "st", "cc", "sq"):
            self.sems[s] = es.enter_context(nc.semaphore("sem_" + s))
        self.cnt = {s: 0 for s in self.sems}
        self.last_wait = {e: {s: 0 for s in self.sems} for e in self.ENGINES}
        self.eng_sem = {"tensor": "pe", "vector": "dve", "scalar": "act",
                        "gpsimd": "gps", "sync": "sq"}

    def _wait(self, eng, deps):
        for d in deps:
            if d is None:
                continue
            s, v = d
            if v <= 0 or self.eng_sem.get(eng) == s:
                continue
            if self.last_wait[eng][s] < v:
                sem = self.sems[s]
                self.ops[eng].append(lambda e, sem=sem, v=v: e.wait_ge(sem, v))
                self.last_wait[eng][s] = v

    def emit(self, eng, fn, deps=(), inc=None, inc_by=1):
        self._wait(eng, deps)
        if inc is None:
            inc = self.eng_sem[eng]
        self.cnt[inc] += inc_by
        tok = (inc, self.cnt[inc])
        sem = self.sems[inc]

        def run(e, fn=fn, sem=sem, n=inc_by):
            fn(e).then_inc(sem, n)
        self.ops[eng].append(run)
        return tok

    def dma(self, eng, out_ap, in_ap, deps=(), sem=None):
        if sem is None:
            sem = "ld" if eng == "sync" else "st"
        return self.emit(eng, lambda e: e.dma_start(out=out_ap, in_=in_ap),
                         deps=deps, inc=sem, inc_by=16)

    def now(self, eng):
        return (self.eng_sem[eng], self.cnt[self.eng_sem[eng]])

    def snap(self):
        return [self.now(e) for e in self.ENGINES] + \
               [("ld", self.cnt["ld"]), ("st", self.cnt["st"])]

    def barrier(self):
        toks = self.snap()
        for e in self.ENGINES:
            self._wait(e, toks)

    def run_all(self, block):
        for eng in self.ENGINES:
            body = self.ops[eng]
            getattr(block, eng)(lambda e, body=body: [fn(e) for fn in body])


def build_nc():
    nc = bass.Bass("TRN2", target_bir_lowering=False)

    x_in = nc.dram_tensor("x_in", [D, T], F32, kind="ExternalInput").ap()
    Wq_in = nc.dram_tensor("Wq", [D, D], F32, kind="ExternalInput").ap()
    Wk_in = nc.dram_tensor("Wk", [D, D], F32, kind="ExternalInput").ap()
    Wv_in = nc.dram_tensor("Wv", [D, D], F32, kind="ExternalInput").ap()
    Wo_in = nc.dram_tensor("Wo", [D, D], F32, kind="ExternalInput").ap()
    Wup_in = nc.dram_tensor("Wup", [I, D], F32, kind="ExternalInput").ap()
    Wgate_in = nc.dram_tensor("Wgate", [I, D], F32, kind="ExternalInput").ap()
    Wdown_in = nc.dram_tensor("Wdown", [D, I], F32, kind="ExternalInput").ap()
    qw_in = nc.dram_tensor("qw", [128, 1], F32, kind="ExternalInput").ap()
    kw_in = nc.dram_tensor("kw", [128, 1], F32, kind="ExternalInput").ap()
    cos_in = nc.dram_tensor("cos4", [128, T], F32, kind="ExternalInput").ap()
    sin_in = nc.dram_tensor("sinc", [128, T], F32, kind="ExternalInput").ap()
    ident_in = nc.dram_tensor("ident", [128, 128], F32, kind="ExternalInput").ap()
    swap_in = nc.dram_tensor("swapm", [128, 128], F32, kind="ExternalInput").ap()
    hind_in = nc.dram_tensor("hind", [128, 2], BF16, kind="ExternalInput").ap()
    hindT_in = nc.dram_tensor("hindT", [2, 128], F32, kind="ExternalInput").ap()
    cmask_in = nc.dram_tensor("cmask", [128, 4 * 512], BF16, kind="ExternalInput").ap()
    hsel_in = nc.dram_tensor("hsel", [128, NH * 16], BF16, kind="ExternalInput").ap()
    sel16_in = nc.dram_tensor("sel16", [16, FC * 128], F32, kind="ExternalInput").ap()
    sel4_in = nc.dram_tensor("sel4", [4, TC * 128], F32, kind="ExternalInput").ap()
    hindsel_in = nc.dram_tensor("hindsel", [128, 128], BF16, kind="ExternalInput").ap()
    y_out = nc.dram_tensor("y_out", [D, T], F32, kind="ExternalOutput").ap()
    dbg = {}
    if DEBUG_TAPS:
        for nm in ("f_U", "f_G", "f_G2", "f_O"):
            dbg[nm] = nc.dram_tensor(nm, [IC, TC, 128, 512], F32,
                                     kind="ExternalOutput").ap()
        dbg["f_oq"] = nc.dram_tensor("f_oq", [IC, TC, 128, 512], BF16,
                                     kind="ExternalOutput").ap()
        dbg["f_wd"] = nc.dram_tensor("f_wd", [IC, 128, FC * 128], BF16,
                                     kind="ExternalOutput").ap()
        dbg["f_Qp"] = nc.dram_tensor("f_Qp", [NH, TC, 64, 512], BF16,
                                     kind="ExternalOutput").ap()
        dbg["f_Kp"] = nc.dram_tensor("f_Kp", [NH, 64, T], BF16,
                                     kind="ExternalOutput").ap()
        dbg["f_Vp"] = nc.dram_tensor("f_Vp", [16, 128, NH * 64], BF16,
                                     kind="ExternalOutput").ap()
        dbg["f_ob"] = nc.dram_tensor("f_ob", [TC, 128, 4096], BF16,
                                     kind="ExternalOutput").ap()
        dbg["f_sum"] = nc.dram_tensor("f_sum", [TC, 16, 512], F32,
                                      kind="ExternalOutput").ap()
        for nm, shp, dt in (("d_U", [128, 512], F32), ("d_G", [128, 512], F32),
                            ("d_G2", [128, 512], F32), ("d_O", [128, 512], F32),
                            ("d_scl", [128, 64], F32), ("d_sx", [4, 512], F32),
                            ("d_a4", [4, 512], F32), ("d_xln", [128, 512], F32),
                            ("d_wup", [128, 1024], BF16),
                            ("d_wdn", [128, 1024], BF16),
                            ("d_xq", [128, 512], BF16),
                            ("d_xq2", [128, 512], BF16),
                            ("d_scl0", [128, 16], F32),
                            ("d_z", [128, 512], F32),
                            ("d_rt", [128, 16], F32),
                            ("d_cst", [1, 16], F32)):
            dbg[nm] = nc.dram_tensor(nm, shp, dt, kind="ExternalOutput").ap()

    wup_bf = nc.dram_tensor("wup_bf", [IC, 128, FC * 128], BF16, kind="Internal").ap()
    wgate_bf = nc.dram_tensor("wgate_bf", [IC, 128, FC * 128], BF16, kind="Internal").ap()
    wdown_bf = nc.dram_tensor("wdown_bf", [IC, 128, FC * 128], BF16, kind="Internal").ap()
    wq_bf = nc.dram_tensor("wq_bf", [FC, 128, D], BF16, kind="Internal").ap()
    wk_bf = nc.dram_tensor("wk_bf", [FC, 128, D], BF16, kind="Internal").ap()
    wv_bf = nc.dram_tensor("wv_bf", [FC, 128, D], BF16, kind="Internal").ap()
    wo_bf = nc.dram_tensor("wo_bf", [FC, 128, D], BF16, kind="Internal").ap()
    U_sp = nc.dram_tensor("U_sp", [IC, TC, 128, 512], F32, kind="Internal").ap()
    G_sp = nc.dram_tensor("G_sp", [IC, TC, 128, 512], F32, kind="Internal").ap()
    G2_sp = nc.dram_tensor("G2_sp", [IC, TC, 128, 512], F32, kind="Internal").ap()
    O_sp = nc.dram_tensor("O_sp", [IC, TC, 128, 512], F32, kind="Internal").ap()
    Qp_sp = nc.dram_tensor("Qp_sp", [NH, TC, 64, 512], BF16, kind="Internal").ap()
    Kp_sp = nc.dram_tensor("Kp_sp", [NH, 64, T], BF16, kind="Internal").ap()
    Vp_sp = nc.dram_tensor("Vp_sp", [16, 128, NH * 64], BF16, kind="Internal").ap()
    NBAR = 10
    cc_in = [nc.dram_tensor(f"ccin{i}", [1, 2], F32, kind="Internal").ap()
             for i in range(NBAR)]
    cc_out = [nc.dram_tensor(f"ccout{i}", [1, 2], F32, kind="Internal").ap()
              for i in range(NBAR)]

    es = ExitStack()
    with es:
        def sb(name, p_, f_, dt=F32):
            return es.enter_context(nc.sbuf_tensor(name, [p_, f_], dt))

        xs = sb("xs", 128, FC * T)
        wst = sb("wst", 128, 4096, BF16)
        lnb = sb("lnb", 128, 2048, BF16)
        oqb = sb("oqb", 128, 1024, BF16)
        scl = sb("scl", 128, 64)
        mxc = sb("mxc", 128, 16)
        cstage = sb("cstage", 1, 16)
        onesrow = sb("onesrow", 1, 128)
        onesb = sb("onesb", 128, 4, BF16)
        ident = sb("identt", 128, 128)
        swapm = sb("swapmt", 128, 128)
        hsel = sb("hselt", 128, NH * 16, BF16)
        sel4 = sb("sel4t", 4, TC * 128)
        hindsel = sb("hindselt", 128, 128, BF16)
        magicc = sb("magicc", 128, 1)
        sx4 = sb("sx4", 4, 512)
        sx24 = sb("sx24", 4, 512)
        a4 = sb("a4", 4, 512)
        b4 = sb("b4", 4, 512)
        scr2 = sb("scr2", 2, 512)
        rtmp = sb("rtmp", 128, 16)
        qw = sb("qwt", 128, 1)
        kw = sb("kwt", 128, 1)

        psum = [es.enter_context(nc.psum_tensor(f"ps{i}", [128, 512], F32))
                for i in range(8)]

        p = Prog(nc, es)

        R_SX, R_SX2, R_MU, R_T0, R_A, R_B = 0, 1, 2, 3, 4, 5
        R_QST, R_KST, R_SUM, R_RCP = 8, 24, 40, 56

        ctoks = []
        for dst, src in ((ident, ident_in), (swapm, swap_in),
                         (qw, qw_in), (kw, kw_in),
                         (hsel, hsel_in), (sel4, sel4_in),
                         (hindsel, hindsel_in)):
            ctoks.append(p.dma("sync", dst[:, :], src[:, :]))
        t_o1 = p.emit("vector", lambda e: e.memset(onesrow[:, :], 1.0))
        t_mg = p.emit("vector", lambda e: e.memset(magicc[:, :], MAGIC))
        t_o2a = p.emit("vector", lambda e: e.memset(onesb[:, 0:1], 1.0))
        t_o2b = p.emit("vector", lambda e: e.memset(onesb[:, 1:3], 0.0))
        t_o2 = p.emit("vector", lambda e: e.memset(onesb[:, 3:4], 1.0))
        ones_tok = [t_o1, t_o2a, t_o2b, t_o2]
        tok_ident = ctoks[0]
        x_tok = []
        for f in range(FC):
            x_tok.append(p.dma("sync", xs[:, f * T:(f + 1) * T],
                               x_in[f * 128:(f + 1) * 128, :]))

        wprep_sc = es.enter_context(nc.sbuf_tensor("wprep", [128, 1120], F32))
        wprep_bf = es.enter_context(nc.sbuf_tensor("wprepb", [128, 1024], BF16))

        wprep_store = {0: None, 1: None}

        def emit_lpbq_prep(src, dst, nrow_chunks, ncols, row_major_dst):
            last = []
            for rc in range(nrow_chunks):
                for c0 in range(0, ncols, 512):
                    pc = min(512, ncols - c0)
                    pb = pc // WBLK
                    wbuf = wprep_sc[:, 0:pc]
                    zbuf = wprep_sc[:, 512:512 + pc]
                    bm = wprep_sc[:, 1024:1024 + pb]
                    s1 = wprep_sc[:, 1040:1040 + pb]
                    s2 = wprep_sc[:, 1056:1056 + pb]
                    s3 = wprep_sc[:, 1072:1072 + pb]
                    r1 = wprep_sc[:, 1088:1088 + pb]
                    r2 = wprep_sc[:, 1104:1104 + pb]
                    d = p.dma("sync", wbuf, src[rc * 128:(rc + 1) * 128, c0:c0 + pc],
                              deps=[p.now("vector"), p.now("tensor")])
                    t_bm = p.emit("vector", lambda e, bm=bm, wbuf=wbuf:
                                  e.tensor_reduce(
                                      bm, wbuf.rearrange("p (b c) -> p b c", c=WBLK),
                                      AX.X, OP.max, apply_absolute_value=True),
                                  deps=[d, p.now("scalar")])
                    t_a1 = p.emit("scalar", lambda e, s1=s1, bm=bm:
                                  e.activation(s1, bm, AF.Identity,
                                               scale=1.0 / WQMAX), deps=[t_bm])
                    t_s = p.emit("vector", lambda e, s2=s2, s1=s1:
                                 e.tensor_scalar_max(s2, s1, 1e-12), deps=[t_a1])
                    t_sk = p.emit("scalar", lambda e, s3=s3, s2=s2:
                                  e.copy(s3, s2), deps=[t_s])
                    t_r1 = p.emit("vector", lambda e, r1=r1, s3=s3:
                                  e.reciprocal(r1, s3), deps=[t_sk])
                    t_r = p.emit("scalar", lambda e, r2=r2, r1=r1:
                                 e.copy(r2, r1), deps=[t_r1])
                    rb = r2.unsqueeze(2).broadcast_to([128, pb, WBLK])
                    sbb = s3.unsqueeze(2).broadcast_to([128, pb, WBLK])
                    t_z = p.emit("vector", lambda e, zbuf=zbuf, wbuf=wbuf, rb=rb:
                                 e.tensor_tensor(
                                     zbuf.rearrange("p (b c) -> p b c", c=WBLK),
                                     wbuf.rearrange("p (b c) -> p b c", c=WBLK),
                                     rb, OP.mult), deps=[t_r])
                    t_c = p.emit("vector", lambda e, zbuf=zbuf:
                                 e.tensor_scalar(zbuf, zbuf, MAGIC, MAGIC + WQMAX,
                                                 OP.add, OP.min), deps=[t_z])
                    t_c2 = p.emit("vector", lambda e, zbuf=zbuf:
                                  e.tensor_scalar(zbuf, zbuf, MAGIC - WQMAX - 1.0,
                                                  MAGIC, OP.max, OP.subtract),
                                  deps=[t_c])
                    t_q = p.emit("vector", lambda e, wbuf=wbuf, zbuf=zbuf, sbb=sbb:
                                 e.tensor_tensor(
                                     wbuf.rearrange("p (b c) -> p b c", c=WBLK),
                                     zbuf.rearrange("p (b c) -> p b c", c=WBLK),
                                     sbb, OP.mult), deps=[t_c2])
                    for k in range(pc // 128):
                        col = c0 + k * 128
                        bank = psum[6 + (k % 2)]
                        t_tp = p.emit("tensor",
                                      lambda e, bank=bank, wbuf=wbuf, k=k:
                                      e.transpose(bank[:, 0:128],
                                                  wbuf[:, k * 128:(k + 1) * 128],
                                                  ident[:, :]),
                                      deps=[t_q, tok_ident, p.now("scalar")])
                        t_ev = p.emit("scalar",
                                      lambda e, bank=bank, k=k:
                                      e.copy(wprep_bf[:, (k % 2) * 128:
                                                      (k % 2) * 128 + 128],
                                             bank[:, 0:128]),
                                      deps=[t_tp, wprep_store[k % 2]])
                        if row_major_dst:
                            dsl = dst[rc, :, col:col + 128]
                        else:
                            dsl = dst[col // 128, :, rc * 128:(rc + 1) * 128]
                        dstore = p.dma(
                            "gpsimd", dsl,
                            wprep_bf[:, (k % 2) * 128:(k % 2) * 128 + 128],
                            deps=[t_ev])
                        wprep_store[k % 2] = dstore
                        last.append(dstore)
            return last[-1]

        def emit_bf_prep(src, dst, per_c):
            last = None
            for f in range(FC):
                wbuf = wprep_sc[:, 0:1024]
                d = p.dma("sync", wbuf, src[f * 128:(f + 1) * 128, :],
                          deps=[p.now("gpsimd"), p.now("vector"),
                                p.now("tensor"), p.now("scalar")])
                t_c = p.emit("gpsimd", lambda e, wbuf=wbuf:
                             e.tensor_copy(wprep_bf[:, 0:1024], wbuf),
                             deps=[d, ("st", p.cnt["st"])])
                if per_c:
                    for c in range(FC):
                        last = p.dma("gpsimd", dst[c, :, f * 128:(f + 1) * 128],
                                     wprep_bf[:, c * 128:(c + 1) * 128],
                                     deps=[t_c])
                else:
                    last = p.dma("gpsimd", dst[f, :, :], wprep_bf[:, 0:1024],
                                 deps=[t_c])
            return last

        tok_wup = emit_lpbq_prep(Wup_in, wup_bf, IC, D, True)
        tok_wgate = emit_lpbq_prep(Wgate_in, wgate_bf, IC, D, True)
        tok_wdown = emit_lpbq_prep(Wdown_in, wdown_bf, FC, I, False)
        tok_wq = emit_bf_prep(Wq_in, wq_bf, True)
        tok_wk = emit_bf_prep(Wk_in, wk_bf, True)
        tok_wv = emit_bf_prep(Wv_in, wv_bf, False)
        tok_wo = emit_bf_prep(Wo_in, wo_bf, False)

        bar_state = [0]

        def emit_allreduce(vals):
            bi = bar_state[0]
            bar_state[0] += 1
            n = len(vals)
            tts = [("st", p.cnt["st"])]
            for j, (tok, col) in enumerate(vals):
                t_tp = p.emit("tensor", lambda e, col=col, j=j:
                              e.transpose(psum[0][0:1, j * 128:(j + 1) * 128],
                                          mxc[:, col:col + 1], ident[:, :]),
                              deps=[tok, tok_ident])
                t_rd = p.emit("vector", lambda e, j=j:
                              e.reduce_max(cstage[0:1, j:j + 1],
                                           psum[0][0:1, j * 128:(j + 1) * 128],
                                           axis=AX.X), deps=[t_tp])
                tts.append(t_rd)
            d1 = p.dma("gpsimd", cc_in[bi][0:1, 0:n], cstage[0:1, 0:n], deps=tts)
            t_cc = p.emit("gpsimd", lambda e, bi=bi, n=n:
                          e.collective_compute(
                              "AllReduce", OP.max,
                              replica_groups=[list(range(NCORE))],
                              ins=[cc_in[bi][0:1, 0:n]],
                              outs=[cc_out[bi][0:1, 0:n]]),
                          deps=[d1], inc="cc")
            d2 = p.dma("gpsimd", cstage[0:1, 4:4 + n], cc_out[bi][0:1, 0:n],
                       deps=[t_cc, p.now("vector")])
            t_bc = p.emit("tensor", lambda e:
                          e.matmul(psum[0][:, 0:16], onesrow[0:1, :],
                                   cstage[0:1, 0:16], start=True, stop=True),
                          deps=[d2] + ones_tok)
            c0 = 16 * (bi % 4)
            rs = 4 * (bi % 4)
            t_cp = p.emit("scalar", lambda e, c0=c0:
                          e.copy(scl[:, c0:c0 + 16], psum[0][:, 0:16]),
                          deps=[t_bc, p.now("vector")])
            t_s = p.emit("vector", lambda e, c0=c0, rs=rs, n=n:
                         e.tensor_scalar(rtmp[:, rs:rs + n],
                                         scl[:, c0 + 4:c0 + 4 + n],
                                         1.0 / QMAX, 1e-12, OP.mult, OP.max),
                         deps=[t_cp])
            t_sc = p.emit("scalar", lambda e, c0=c0, rs=rs, n=n:
                          e.copy(scl[:, c0 + 8:c0 + 8 + n], rtmp[:, rs:rs + n]),
                          deps=[t_s])
            t_n = p.emit("scalar", lambda e, c0=c0, n=n:
                         e.activation(scl[:, c0 + 12:c0 + 12 + n],
                                      scl[:, c0 + 8:c0 + 8 + n],
                                      AF.Identity, scale=-MAGIC),
                         deps=[t_sc])
            t_r = p.emit("vector", lambda e, c0=c0, rs=rs, n=n:
                         e.reciprocal(rtmp[:, rs + 2:rs + 2 + n],
                                      scl[:, c0 + 8:c0 + 8 + n]),
                         deps=[t_sc])
            t_rc = p.emit("scalar", lambda e, c0=c0, rs=rs, n=n:
                          e.copy(scl[:, c0 + 10:c0 + 10 + n],
                                 rtmp[:, rs + 2:rs + 2 + n]),
                          deps=[t_r])
            out = {}
            for j in range(n):
                out[j] = (scl[:, c0 + 10 + j:c0 + 11 + j],
                          scl[:, c0 + 8 + j:c0 + 9 + j],
                          scl[:, c0 + 12 + j:c0 + 13 + j],
                          t_rc)
            return out

        class MaxChain:

            def __init__(self, run_col, sc0, init_deps):
                self.run = run_col
                self.sc = sc0
                self.k = 0
                self.pend = None
                self.tok = p.emit("vector", lambda e:
                                  e.memset(mxc[:, self.run:self.run + 1], 0.0),
                                  deps=list(init_deps))
                self.seen = {}

            def add(self, tile_ap, deps):
                col = self.sc + (self.k % 2)
                rt_ = p.emit("vector", lambda e, tile_ap=tile_ap, col=col:
                             e.reduce_max(mxc[:, col:col + 1], tile_ap,
                                          axis=AX.X,
                                          apply_absolute_value=True),
                             deps=list(deps) + [self.seen.get(col)])
                self._combine_pending()
                self.pend = (col, rt_)
                self.k += 1
                return rt_

            def _combine_pending(self):
                if self.pend is None:
                    return
                col, rt_ = self.pend
                self.tok = p.emit("vector", lambda e, col=col:
                                  e.tensor_max(mxc[:, self.run:self.run + 1],
                                               mxc[:, self.run:self.run + 1],
                                               mxc[:, col:col + 1]),
                                  deps=[rt_, self.tok])
                self.seen[col] = self.tok
                self.pend = None

            def flush(self, dummy_ap):
                if self.pend is not None:
                    p.emit("vector", lambda e, dummy_ap=dummy_ap:
                           e.reduce_max(mxc[:, 15:16], dummy_ap, axis=AX.X,
                                        apply_absolute_value=True),
                           deps=[self.pend[1]])
                    self._combine_pending()
                return self.tok

        def emit_ln(dep_toks):
            ev_toks = []
            scr_last = [None]
            for tc in range(TC):
                mmx = mms = None
                for f in range(FC):
                    z = (f % 2) * 512
                    sl = xs[:, f * T + tc * 512: f * T + (tc + 1) * 512]
                    t_bc = p.emit("gpsimd", lambda e, sl=sl, z=z:
                                  e.tensor_copy(lnb[:, z:z + 512], sl),
                                  deps=dep_toks + [p.now("tensor")])
                    t_sq = p.emit("scalar", lambda e, sl=sl, z=z:
                                  e.activation(lnb[:, 1024 + z:1024 + z + 512],
                                               sl, AF.Square),
                                  deps=dep_toks + [p.now("tensor")])
                    mmx = p.emit("tensor", lambda e, f=f, z=z:
                                 e.matmul(psum[0][0:2, 0:512], onesb[:, 0:2],
                                          lnb[:, z:z + 512],
                                          start=(f == 0), stop=False),
                                 deps=[t_bc] + ones_tok)
                    mms = p.emit("tensor", lambda e, f=f, z=z:
                                 e.matmul(psum[0][0:2, 0:512], onesb[:, 2:4],
                                          lnb[:, 1024 + z:1024 + z + 512],
                                          start=False, stop=(f == FC - 1)),
                                 deps=[t_sq])
                t_ev = p.emit("scalar", lambda e:
                              e.copy(scr2[0:2, 0:512], psum[0][0:2, 0:512]),
                              deps=[mmx, mms, scr_last[0]])
                d1 = p.dma("sync", sx4[tc:tc + 1, :], scr2[0:1, :], deps=[t_ev])
                d2 = p.dma("sync", sx24[tc:tc + 1, :], scr2[1:2, :], deps=[t_ev])
                scr_last[0] = d2
                ev_toks += [d1, d2, t_ev]
            t_mu = p.emit("vector", lambda e:
                          e.tensor_scalar_mul(sx4[:, :], sx4[:, :], 1.0 / D),
                          deps=ev_toks + [p.now("tensor")])
            t_e2 = p.emit("vector", lambda e:
                          e.tensor_scalar_mul(sx24[:, :], sx24[:, :], 1.0 / D),
                          deps=ev_toks)
            t_m2 = p.emit("vector", lambda e:
                          e.tensor_tensor(b4[:, :], sx4[:, :], sx4[:, :],
                                          OP.mult), deps=[t_mu])
            t_v = p.emit("vector", lambda e:
                         e.tensor_tensor(sx24[:, :], sx24[:, :], b4[:, :],
                                         OP.subtract), deps=[t_m2, t_e2])
            t_ve = p.emit("vector", lambda e:
                          e.tensor_scalar_add(sx24[:, :], sx24[:, :], EPS_LN),
                          deps=[t_v])
            t_rc = p.emit("vector", lambda e:
                          e.reciprocal(b4[:, :], sx24[:, :]), deps=[t_ve])
            t_a = p.emit("scalar", lambda e:
                         e.activation(a4[:, :], b4[:, :], AF.Sqrt),
                         deps=[t_rc])
            t_b = p.emit("vector", lambda e:
                         e.scalar_tensor_tensor(b4[:, :], sx4[:, :], -1.0,
                                                a4[:, :], OP.mult, OP.mult),
                         deps=[t_a])
            return t_a, t_b

        def emit_ab_bcast(t_a, t_b, tc, extra=()):
            t_ab = p.emit("tensor", lambda e, tc=tc:
                          e.matmul(psum[1][:, 0:512],
                                   sel4[:, tc * 128:(tc + 1) * 128],
                                   a4[:, :], start=True, stop=True),
                          deps=[t_a, ctoks[5]] + list(extra))
            t_bb = p.emit("tensor", lambda e, tc=tc:
                          e.matmul(psum[2][:, 0:512],
                                   sel4[:, tc * 128:(tc + 1) * 128],
                                   b4[:, :], start=True, stop=True),
                          deps=[t_b] + list(extra))
            return t_ab, t_bb

        ffn_idx = [0]

        def emit_ffn(dep_toks):
            fes = ExitStack()
            tag = str(ffn_idx[0])
            ffn_idx[0] += 1
            with fes:
                xln = fes.enter_context(
                    nc.sbuf_tensor("xln" + tag, [128, FC * T], F32))
                xqb = fes.enter_context(
                    nc.sbuf_tensor("xqb" + tag, [128, FC * T], BF16))
                stg = fes.enter_context(
                    nc.sbuf_tensor("stg" + tag, [128, 3072], F32))

                t_a, t_b = emit_ln(dep_toks)

                mc0 = MaxChain(0, 8, [p.now("tensor")])
                xln_done = []
                for tc in range(TC):
                    t_ab, t_bb = emit_ab_bcast(t_a, t_b, tc,
                                               extra=[p.now("vector")])
                    for f in range(FC):
                        xsl = xs[:, f * T + tc * 512: f * T + (tc + 1) * 512]
                        lsl = xln[:, f * T + tc * 512: f * T + (tc + 1) * 512]
                        t1 = p.emit("vector", lambda e, xsl=xsl, lsl=lsl:
                                    e.tensor_tensor(lsl, xsl,
                                                    psum[1][:, 0:512], OP.mult),
                                    deps=[t_ab] + dep_toks)
                        t2 = p.emit("vector", lambda e, lsl=lsl:
                                    e.tensor_tensor(lsl, lsl,
                                                    psum[2][:, 0:512], OP.add),
                                    deps=[t_bb])
                        mc0.add(lsl, [t2])
                        xln_done.append(t2)
                last_red = mc0.flush(xln[:, 0:512])
                sc0 = emit_allreduce([(last_red, 0), (last_red, 0)])
                r0, s0, n0, t_sc0 = sc0[0]
                if DEBUG_TAPS and dbg.get("_ffn_done") is None:
                    dbg["_ffn_done"] = True
                    p.dma("gpsimd", dbg["d_scl0"][:, :], scl[:, 0:16],
                          deps=[t_sc0])
                    p.dma("gpsimd", dbg["d_rt"][:, :], rtmp[:, :],
                          deps=[t_sc0])
                    p.dma("gpsimd", dbg["d_xln"][:, :], xln[:, 0:512],
                          deps=[t_sc0])
                    p.dma("gpsimd", dbg["d_cst"][:, :], cstage[0:1, :],
                          deps=[t_sc0])


                t_xq = None
                for tc in range(TC):
                    for f in range(FC):
                        lsl = xln[:, f * T + tc * 512: f * T + (tc + 1) * 512]
                        qsl = xqb[:, f * T + tc * 512: f * T + (tc + 1) * 512]
                        zsl = stg[:, (f % 2) * 512:(f % 2) * 512 + 512]
                        tz = p.emit("scalar", lambda e, zsl=zsl, lsl=lsl, r0=r0:
                                    e.activation(zsl, lsl, AF.Identity,
                                                 scale=r0, bias=magicc[:, 0:1]),
                                    deps=[t_sc0, t_mg, p.now("vector")])
                        if DEBUG_TAPS and tc == 0 and f == 0 and \
                                dbg.get("_z_done") is None:
                            dbg["_z_done"] = True
                            tz = p.dma("gpsimd", dbg["d_z"][:, :], zsl,
                                       deps=[tz])
                        t_xq = p.emit("vector",
                                      lambda e, qsl=qsl, zsl=zsl, s0=s0:
                                      e.tensor_scalar(qsl, zsl, MAGIC, s0,
                                                      OP.subtract, OP.mult),
                                      deps=[tz])

                if DEBUG_TAPS and dbg.get("_xq_done") is None:
                    dbg["_xq_done"] = True
                    p.dma("gpsimd", dbg["d_xq"][:, :], xqb[:, 0:512],
                          deps=[t_xq])
                    p.dma("gpsimd", dbg["d_xq2"][:, :], xqb[:, T:T + 512],
                          deps=[t_xq])
                mcU = MaxChain(1, 10, [p.now("tensor")])
                mcG = MaxChain(2, 12, [])
                spill_tok = {}
                sa0 = [p.now("scalar"), p.now("vector")]
                bank_free = {3: sa0, 4: sa0, 5: sa0, 6: sa0}
                wz_free = {0: sa0, 1: sa0}
                rU2 = rG2 = None
                for ic in range(IC):
                    par = ic % 2
                    wu = wst[:, par * 2048: par * 2048 + 1024]
                    wg = wst[:, par * 2048 + 1024: par * 2048 + 2048]
                    dwu = p.dma("sync", wu, wup_bf[ic, :, :],
                                deps=list(wz_free[par]) + [tok_wup, tok_wgate])
                    dwg = p.dma("sync", wg, wgate_bf[ic, :, :],
                                deps=list(wz_free[par]))
                    mm_last = None
                    for tc in range(TC):
                        bU = 3 + 2 * (tc % 2)
                        bG = 4 + 2 * (tc % 2)
                        mmU = mmG = None
                        for f in range(FC):
                            qsl = xqb[:, f * T + tc * 512: f * T + (tc + 1) * 512]
                            mmU = p.emit("tensor",
                                  lambda e, bU=bU, wu=wu, f=f, qsl=qsl:
                                  e.matmul(psum[bU][:, 0:512],
                                           wu[:, f * 128:(f + 1) * 128], qsl,
                                           start=(f == 0), stop=(f == FC - 1)),
                                  deps=[dwu, t_xq] + list(bank_free[bU]))
                        for f in range(FC):
                            qsl = xqb[:, f * T + tc * 512: f * T + (tc + 1) * 512]
                            mmG = p.emit("tensor",
                                  lambda e, bG=bG, wg=wg, f=f, qsl=qsl:
                                  e.matmul(psum[bG][:, 0:512],
                                           wg[:, f * 128:(f + 1) * 128], qsl,
                                           start=(f == 0), stop=(f == FC - 1)),
                                  deps=[dwg, t_xq] + list(bank_free[bG]))
                        sU = stg[:, 1024 + (tc % 2) * 512: 1536 + (tc % 2) * 512]
                        sG = stg[:, 2048 + (tc % 2) * 512: 2560 + (tc % 2) * 512]
                        eU = p.emit("scalar", lambda e, sU=sU, bU=bU:
                                    e.copy(sU, psum[bU][:, 0:512]),
                                    deps=[mmU, p.now("vector"),
                                          spill_tok.get(("U", tc % 2))])
                        eG = p.emit("scalar", lambda e, sG=sG, bG=bG:
                                    e.copy(sG, psum[bG][:, 0:512]),
                                    deps=[mmG, p.now("vector"),
                                          spill_tok.get(("G", tc % 2))])
                        bank_free[bU] = [eU]
                        bank_free[bG] = [eG]
                        rU = mcU.add(sU, [eU])
                        rG = mcG.add(sG, [eG])
                        spill_tok[("U", tc % 2)] = p.dma(
                            "gpsimd", U_sp[ic, tc, :, :], sU, deps=[eU, rU])
                        spill_tok[("G", tc % 2)] = p.dma(
                            "gpsimd", G_sp[ic, tc, :, :], sG, deps=[eG, rG])
                        mm_last = mmG
                    wz_free[par] = [mm_last]
                rU2 = mcU.flush(xqb[:, 0:512])
                rG2 = mcG.flush(xqb[:, 0:512])
                sc12 = emit_allreduce([(rU2, 1), (rG2, 2)])
                r1, s1, n1, t_s1 = sc12[0]
                r2, s2, n2, t_s2 = sc12[1]

                start_b = [("st", p.cnt["st"]), p.now("scalar"),
                           p.now("vector"), t_s2]
                mc3 = MaxChain(3, 8, [p.now("tensor")])
                zone_last = {0: start_b, 1: start_b}
                for k in range(IC * TC):
                    ic, tc = divmod(k, TC)
                    par = k % 2
                    z0 = stg[:, par * 1536: par * 1536 + 512]
                    z1 = stg[:, par * 1536 + 512: par * 1536 + 1024]
                    z2 = stg[:, par * 1536 + 1024: par * 1536 + 1536]
                    dG = p.dma("sync", z0, G_sp[ic, tc, :, :],
                               deps=list(zone_last[par]))
                    tz = p.emit("scalar", lambda e, z1=z1, z0=z0, r2=r2:
                                e.activation(z1, z0, AF.Identity,
                                             scale=r2, bias=magicc[:, 0:1]),
                                deps=[dG, t_s2, t_mg])
                    tq = p.emit("vector", lambda e, z0=z0, z1=z1, s2=s2:
                                e.tensor_scalar(z0, z1, MAGIC, s2,
                                                OP.subtract, OP.mult),
                                deps=[tz])
                    tsg = p.emit("scalar", lambda e, z2=z2, z0=z0:
                                 e.activation(z2, z0, AF.Sigmoid), deps=[tq])
                    tz2 = p.emit("scalar", lambda e, z1=z1, z2=z2:
                                 e.activation(z1, z2, AF.Identity,
                                              scale=65536.0,
                                              bias=magicc[:, 0:1]),
                                 deps=[tsg, t_mg])
                    tst = p.emit("vector", lambda e, z2=z2, z1=z1:
                                 e.tensor_scalar(z2, z1, MAGIC, U16MAX,
                                                 OP.subtract, OP.min),
                                 deps=[tz2])
                    tg2 = p.emit("vector", lambda e, z1=z1, z0=z0, z2=z2:
                                 e.scalar_tensor_tensor(z1, z0, 1.0 / 65536.0,
                                                        z2, OP.mult, OP.mult),
                                 deps=[tst])
                    tr = mc3.add(z1, [tg2])
                    dS = p.dma("gpsimd", G2_sp[ic, tc, :, :], z1, deps=[tg2])
                    zone_last[par] = (dS, tr, tsg)
                r3t = mc3.flush(xqb[:, 0:512])
                sc3 = emit_allreduce([(r3t, 3), (r3t, 3)])
                r3, s3, n3, t_s3 = sc3[0]

                start_c = [("st", p.cnt["st"]), p.now("scalar"),
                           p.now("vector"), t_s3]
                mc4 = MaxChain(4, 8, [p.now("tensor")])
                zone_last = {0: start_c, 1: start_c}
                for k in range(IC * TC):
                    ic, tc = divmod(k, TC)
                    par = k % 2
                    z0 = stg[:, par * 1536: par * 1536 + 512]
                    z1 = stg[:, par * 1536 + 512: par * 1536 + 1024]
                    z2 = stg[:, par * 1536 + 1024: par * 1536 + 1536]
                    z3 = z1
                    dU = p.dma("sync", z0, U_sp[ic, tc, :, :],
                               deps=list(zone_last[par]) + [t_s1])
                    dG2 = p.dma("sync", z2, G2_sp[ic, tc, :, :],
                                deps=list(zone_last[par]))
                    tz = p.emit("scalar", lambda e, z1=z1, z0=z0, r1=r1:
                                e.activation(z1, z0, AF.Identity,
                                             scale=r1, bias=magicc[:, 0:1]),
                                deps=[dU, t_s1, t_mg])
                    tuq = p.emit("vector", lambda e, z0=z0, z1=z1, s1=s1:
                                 e.tensor_scalar(z0, z1, MAGIC, s1,
                                                 OP.subtract, OP.mult),
                                 deps=[tz])
                    tz3 = p.emit("scalar", lambda e, z1=z1, z2=z2, r3=r3:
                                 e.activation(z1, z2, AF.Identity,
                                              scale=r3, bias=magicc[:, 0:1]),
                                 deps=[dG2, t_s3, tuq])
                    tgq = p.emit("vector", lambda e, z2=z2, z1=z1, s3=s3:
                                 e.tensor_scalar(z2, z1, MAGIC, s3,
                                                 OP.subtract, OP.mult),
                                 deps=[tz3])
                    to = p.emit("vector", lambda e, z3=z3, z0=z0, z2=z2:
                                e.tensor_tensor(z3, z0, z2, OP.mult),
                                deps=[tgq])
                    tr = mc4.add(z3, [to])
                    dO = p.dma("gpsimd", O_sp[ic, tc, :, :], z3, deps=[to])
                    zone_last[par] = (dO, tr, tuq, tgq)
                r4t = mc4.flush(xqb[:, 0:512])
                sc4 = emit_allreduce([(r4t, 4), (r4t, 4)])
                r4, s4, n4, t_s4 = sc4[0]

                start_d = [("st", p.cnt["st"]), p.now("scalar"),
                           p.now("vector"), p.now("tensor"), t_s4]
                zone_last = {0: start_d, 1: start_d}
                wz_free = {0: start_d, 1: start_d}
                res_toks = []
                for tc in range(TC):
                    mm_last = None
                    for ic in range(IC):
                        k = tc * IC + ic
                        par = k % 2
                        z0 = stg[:, par * 1024: par * 1024 + 512]
                        z1 = stg[:, par * 1024 + 512: par * 1024 + 1024]
                        oq = oqb[:, par * 512: par * 512 + 512]
                        wd = wst[:, par * 2048: par * 2048 + 1024]
                        dO = p.dma("sync", z0, O_sp[ic, tc, :, :],
                                   deps=list(zone_last[par]))
                        dW = p.dma("sync", wd, wdown_bf[ic, :, :],
                                   deps=list(wz_free[par]) + [tok_wdown])
                        tz = p.emit("scalar", lambda e, z1=z1, z0=z0, r4=r4:
                                    e.activation(z1, z0, AF.Identity,
                                                 scale=r4, bias=magicc[:, 0:1]),
                                    deps=[dO, t_s4, t_mg])
                        toq = p.emit("vector",
                                     lambda e, oq=oq, z1=z1, s4=s4:
                                     e.tensor_scalar(oq, z1, MAGIC, s4,
                                                     OP.subtract, OP.mult),
                                     deps=[tz, p.now("tensor")])
                        dbg_tok = []
                        if DEBUG_TAPS:
                            dbg_tok = [p.dma("gpsimd", dbg["f_oq"][ic, tc], oq,
                                             deps=[toq])]
                        for h in reversed(range(FC)):
                            mm_last = p.emit("tensor",
                                  lambda e, h=h, wd=wd, oq=oq, ic=ic:
                                  e.matmul(psum[h][:, 0:512],
                                           wd[:, h * 128:(h + 1) * 128], oq,
                                           start=(ic == 0), stop=(ic == IC - 1)),
                                  deps=[dW, toq] +
                                       (res_toks[-FC:] if ic == 0 else []))
                        zone_last[par] = tuple([toq, tz] + dbg_tok)
                        wz_free[par] = (mm_last,)
                    for h in range(FC):
                        xsl = xs[:, h * T + tc * 512: h * T + (tc + 1) * 512]
                        res_toks.append(p.emit("vector",
                                       lambda e, xsl=xsl, h=h:
                                       e.scalar_tensor_tensor(
                                           xsl, psum[h][:, 0:512], 0.5, xsl,
                                           OP.mult, OP.add),
                                       deps=[mm_last]))
                p.barrier()
                return [p.now("vector")]

        def emit_attn(dep_toks):
            aes = ExitStack()
            with aes:
                hb = aes.enter_context(nc.sbuf_tensor("hb", [128, 8192], BF16))
                ascr = aes.enter_context(nc.sbuf_tensor("ascr", [128, 5632], F32))
                ptb = aes.enter_context(nc.sbuf_tensor("ptb", [128, 2048], BF16))
                ob = aes.enter_context(nc.sbuf_tensor("ob", [128, 4096], BF16))
                ropeb = aes.enter_context(nc.sbuf_tensor("ropeb", [128, 1024], BF16))
                qt = aes.enter_context(nc.sbuf_tensor("qt", [64, 1024], BF16))
                kt = aes.enter_context(nc.sbuf_tensor("kt", [64, 4096], BF16))
                vt = aes.enter_context(nc.sbuf_tensor("vt", [128, 2048], BF16))

                qst = aes.enter_context(nc.sbuf_tensor("qst", [16, 512], F32))
                kst = aes.enter_context(nc.sbuf_tensor("kst", [16, 512], F32))
                sumst = aes.enter_context(nc.sbuf_tensor("sumst", [16, 512], F32))
                rcpst = aes.enter_context(nc.sbuf_tensor("rcpst", [16, 512], F32))
                rsc = aes.enter_context(nc.sbuf_tensor("rsc", [16, 512], F32))
                sumacc = aes.enter_context(nc.sbuf_tensor("sumacc", [16, 512], F32))
                sel16 = aes.enter_context(
                    nc.sbuf_tensor("sel16t", [16, FC * 128], F32))
                cos4 = aes.enter_context(nc.sbuf_tensor("cos4t", [128, T], F32))
                sinc = aes.enter_context(nc.sbuf_tensor("sinct", [128, T], F32))
                cmask = aes.enter_context(
                    nc.sbuf_tensor("cmaskt", [128, 4 * 512], BF16))
                tok_cos = p.dma("sync", cos4[:, :], cos_in[:, :],
                                deps=[p.now("vector"), p.now("scalar")])
                tok_sin = p.dma("sync", sinc[:, :], sin_in[:, :])
                tok_cm = p.dma("sync", cmask[:, :], cmask_in[:, :])
                tok_sel = p.dma("sync", sel16[:, :], sel16_in[:, :])

                t_a, t_b = emit_ln(dep_toks)

                last_spill = []
                for tc in range(TC):
                    par = tc % 2
                    hz = hb[:, par * 4096:(par + 1) * 4096]
                    t_ab, t_bb = emit_ab_bcast(t_a, t_b, tc,
                                               extra=[p.now("vector")])
                    h_toks = []
                    for f in range(FC):
                        xsl = xs[:, f * T + tc * 512: f * T + (tc + 1) * 512]
                        tmp = ascr[:, 4096:4608]
                        hsl = hz[:, f * 512:(f + 1) * 512]
                        t1 = p.emit("vector", lambda e, tmp=tmp, xsl=xsl:
                                    e.tensor_tensor(tmp, xsl,
                                                    psum[1][:, 0:512], OP.mult),
                                    deps=[t_ab] + dep_toks)
                        t2 = p.emit("vector", lambda e, hsl=hsl, tmp=tmp:
                                    e.tensor_tensor(hsl, tmp,
                                                    psum[2][:, 0:512], OP.add),
                                    deps=[t_bb])
                        h_toks.append(t2)

                    v_sync = [p.now("vector"), p.now("scalar")]
                    v_mms = {}
                    for f in range(FC):
                        wz = wst[:, 2048 + (f % 2) * 1024:
                                 2048 + (f % 2) * 1024 + 1024]
                        dW = p.dma("sync", wz, wv_bf[f, :, :],
                                   deps=[tok_wv, p.now("tensor")])
                        for tsub in range(4):
                            for half in range(2):
                                bank = tsub * 2 + half
                                v_mms[(tsub, half)] = p.emit(
                                    "tensor",
                                    lambda e, bank=bank, hz=hz, f=f, tsub=tsub,
                                    half=half, wz=wz:
                                    e.matmul(
                                        psum[bank][:, 0:512],
                                        hz[:, f * 512 + tsub * 128:
                                           f * 512 + (tsub + 1) * 128],
                                        wz[:, half * 512:(half + 1) * 512],
                                        start=(f == 0), stop=(f == FC - 1)),
                                    deps=[dW, h_toks[f]] +
                                         (v_sync if f == 0 else []))
                    vsp = {}
                    for tsub in range(4):
                        for half in range(2):
                            tt = tc * 4 + tsub
                            bank = tsub * 2 + half
                            vz = lnb[:, half * 512:(half + 1) * 512]
                            ev = p.emit("scalar",
                                 lambda e, vz=vz, bank=bank:
                                 e.copy(vz, psum[bank][:, 0:512]),
                                 deps=[v_mms[(tsub, half)], vsp.get(half),
                                       p.now("vector")])
                            vsp[half] = p.dma(
                                "gpsimd", Vp_sp[tt, :, half * 512:(half + 1) * 512],
                                vz, deps=[ev])
                            last_spill.append(vsp[half])

                    for which, wdram, wtok, wgt in (
                            (0, wq_bf, tok_wq, qw), (1, wk_bf, tok_wk, kw)):
                        ev_raw = {}
                        bank_free = {3: [p.now("scalar")], 4: [p.now("scalar")]}
                        st_prev = [p.now("vector"), p.now("scalar")]
                        for c in range(FC):
                            wz = wst[:, (c % 2) * 1024:(c % 2) * 1024 + 1024]
                            dW = p.dma("sync", wz, wdram[c, :, :],
                                       deps=[wtok, p.now("tensor")])
                            bank = 3 + (c % 2)
                            mm = None
                            for f in range(FC):
                                hsl = hz[:, f * 512:(f + 1) * 512]
                                mm = p.emit("tensor",
                                     lambda e, bank=bank, wz=wz, f=f, hsl=hsl:
                                     e.matmul(psum[bank][:, 0:512],
                                              wz[:, f * 128:(f + 1) * 128], hsl,
                                              start=(f == 0), stop=(f == FC - 1)),
                                     deps=[dW, h_toks[f]] + bank_free[bank])
                            zraw = ascr[:, c * 512:(c + 1) * 512]
                            ev = p.emit("scalar", lambda e, zraw=zraw, bank=bank:
                                        e.copy(zraw, psum[bank][:, 0:512]),
                                        deps=[mm, p.now("vector")])
                            sq_ = p.emit("scalar",
                                         lambda e, bank=bank, c=c:
                                         e.activation(
                                             lnb[:, 1024 + (c % 2) * 512:
                                                 1536 + (c % 2) * 512],
                                             psum[bank][:, 0:512], AF.Square),
                                         deps=[mm])
                            bank_free[bank] = [ev, sq_]
                            smm = p.emit("tensor", lambda e, c=c:
                                         e.matmul(
                                             psum[5][0:16, 0:512],
                                             hindsel[:, c * 16:(c + 1) * 16],
                                             lnb[:, 1024 + (c % 2) * 512:
                                                 1536 + (c % 2) * 512],
                                             start=(c == 0), stop=(c == FC - 1)),
                                         deps=[sq_, ctoks[6]] +
                                              (st_prev if c == 0 else []))
                            ev_raw[c] = ev
                        stq = qst if which == 0 else kst
                        sev = p.emit("scalar", lambda e, stq=stq:
                                     e.copy(stq[0:16, 0:512],
                                            psum[5][0:16, 0:512]),
                                     deps=[smm, p.now("vector")])
                        t_r1 = p.emit("vector", lambda e, stq=stq:
                                      e.tensor_scalar(
                                          stq[:, :], stq[:, :],
                                          1.0 / HD, EPS_RMS, OP.mult, OP.add),
                                      deps=[sev])
                        t_r2 = p.emit("vector", lambda e, stq=stq:
                                      e.reciprocal(rsc[:, :], stq[:, :]),
                                      deps=[t_r1, p.now("scalar")])
                        scale_sq = 0.125 * 0.125 if which == 0 else 1.0
                        t_r3 = p.emit("scalar", lambda e, stq=stq,
                                      scale_sq=scale_sq:
                                      e.activation(stq[:, :], rsc[:, :],
                                                   AF.Sqrt, scale=scale_sq),
                                      deps=[t_r2])
                        rb_free = [p.now("vector")]
                        sw_free = {0: [p.now("scalar")], 1: [p.now("scalar")]}
                        for c in range(FC):
                            zraw = ascr[:, c * 512:(c + 1) * 512]
                            nqz = ascr[:, 4608 + (c % 2) * 512:
                                       5120 + (c % 2) * 512]
                            t_rb = p.emit("tensor", lambda e, c=c, stq=stq:
                                          e.matmul(psum[5][:, 0:512],
                                                   sel16[:, c * 128:(c + 1) * 128],
                                                   stq[:, :],
                                                   start=True, stop=True),
                                          deps=[t_r3, tok_sel] + rb_free)
                            t_nq = p.emit("vector",
                                   lambda e, nqz=nqz, zraw=zraw, wgt=wgt:
                                   e.scalar_tensor_tensor(
                                       nqz, zraw, wgt[:, 0:1],
                                       psum[5][:, 0:512], OP.mult, OP.mult),
                                   deps=[t_rb, ev_raw[c]])
                            rb_free = [t_nq]
                            t_sw = p.emit("tensor", lambda e, nqz=nqz, c=c:
                                          e.matmul(psum[6 + c % 2][:, 0:512],
                                                   swapm[:, :], nqz,
                                                   start=True, stop=True),
                                          deps=[t_nq, ctoks[1]] +
                                               sw_free[c % 2])
                            tt1 = ascr[:, 4096:4608]
                            t_c1 = p.emit("vector",
                                   lambda e, tt1=tt1, nqz=nqz, tc=tc:
                                   e.tensor_tensor(
                                       tt1, nqz,
                                       cos4[:, tc * 512:(tc + 1) * 512],
                                       OP.mult),
                                   deps=[t_nq, tok_cos])
                            t_c2 = p.emit("vector",
                                   lambda e, nqz=nqz, tc=tc, c=c:
                                   e.tensor_tensor(
                                       nqz, psum[6 + c % 2][:, 0:512],
                                       sinc[:, tc * 512:(tc + 1) * 512],
                                       OP.mult),
                                   deps=[t_sw, tok_sin, t_c1])
                            rsl = ropeb[:, (c % 2) * 512:(c % 2) * 512 + 512]
                            t_rp = p.emit("vector",
                                   lambda e, rsl=rsl, tt1=tt1, nqz=nqz:
                                   e.tensor_tensor(rsl, tt1, nqz, OP.add),
                                   deps=[t_c2, p.now("gpsimd")])
                            sw_free[c % 2] = [t_c2]
                            for half in range(2):
                                h = 2 * c + half
                                if which == 0:
                                    dsl = Qp_sp[h, tc, :, :]
                                else:
                                    dsl = Kp_sp[h, :, tc * 512:(tc + 1) * 512]
                                last_spill.append(p.dma(
                                    "gpsimd", dsl,
                                    rsl[64 * half:64 * half + 64, :],
                                    deps=[t_rp]))

                if ATTN_PART < 2:
                    p.barrier()
                    return [p.now("vector")]
                spill_done = [("st", p.cnt["st"])]
                res_toks = []
                sum_ev = None
                prev_ob = {}
                for qc in range(TC):
                    kmax = 4 * (qc + 1)
                    ob_done = {}
                    t_ob_prev = None
                    qc_carry = [p.now("vector"), p.now("scalar")]
                    t_sac = p.emit("vector", lambda e: e.memset(sumacc[:, :], 0.0),
                                   deps=qc_carry)
                    t_add = None
                    for h in range(NH):
                        parh = h % 2
                        qz = qt[:, parh * 512:(parh + 1) * 512]
                        kz = kt[:, parh * 2048:(parh + 1) * 2048]
                        vz = vt[:, parh * 1024:(parh + 1) * 1024]
                        guard = prev_ob.get(h, None)
                        dQ = p.dma("sync", qz[0:64, :], Qp_sp[h, qc, :, :],
                                   deps=spill_done + [guard,
                                                      prev_ob.get(h - 2, None)])
                        dK = p.dma("sync", kz[0:64, 0:kmax * 128],
                                   Kp_sp[h, :, 0:kmax * 128],
                                   deps=spill_done + [prev_ob.get(h - 2, None)])
                        dVs = []
                        for kc in range(kmax):
                            dVs.append(p.dma(
                                "sync", vz[:, kc * 64:(kc + 1) * 64],
                                Vp_sp[kc, :, h * 64:(h + 1) * 64],
                                deps=spill_done + [prev_ob.get(h - 2, None)]))
                        mm_o = None
                        sum_mm = None
                        pt_free = {i: None for i in range(4)}
                        sc_free = {3: qc_carry[0], 4: qc_carry[0]}
                        for kc in range(kmax):
                            bank = 3 + (kc % 2)
                            mm_s = p.emit("tensor",
                                   lambda e, bank=bank, kz=kz, qz=qz, kc=kc:
                                   e.matmul(psum[bank][:, 0:512],
                                            kz[0:64, kc * 128:(kc + 1) * 128],
                                            qz[0:64, :],
                                            start=True, stop=True),
                                   deps=[dQ, dK, sc_free[bank]])
                            ptz = ptb[:, (kc % 4) * 512:(kc % 4) * 512 + 512]
                            t_e = p.emit("scalar", lambda e, ptz=ptz, bank=bank:
                                         e.activation(ptz, psum[bank][:, 0:512],
                                                      AF.Exp),
                                         deps=[mm_s, pt_free[kc % 4]])
                            sc_free[bank] = t_e
                            dpos = kc - 4 * qc
                            if dpos >= 0:
                                t_e = p.emit("vector",
                                      lambda e, ptz=ptz, dpos=dpos:
                                      e.tensor_tensor(
                                          ptz, ptz,
                                          cmask[:, dpos * 512:(dpos + 1) * 512],
                                          OP.mult),
                                      deps=[t_e, tok_cm])
                            if ATTN_SUM:
                                sum_mm = p.emit("tensor",
                                    lambda e, h=h, ptz=ptz, kc=kc, kmax=kmax:
                                    e.matmul(psum[7][0:16, 0:512],
                                             hsel[:, h * 16:h * 16 + 16], ptz,
                                             start=(kc == 0),
                                             stop=(kc == kmax - 1)),
                                    deps=[t_e, ctoks[6]] +
                                         ([t_add] if kc == 0 else []))
                            mm_o = p.emit("tensor",
                                   lambda e, vz=vz, ptz=ptz, kc=kc, kmax=kmax:
                                   e.matmul(psum[6][0:64, 0:512],
                                            vz[:, kc * 64:(kc + 1) * 64], ptz,
                                            start=(kc == 0),
                                            stop=(kc == kmax - 1)),
                                   deps=[t_e] + dVs +
                                        ([t_ob_prev] if kc == 0 else []))
                            pt_free[kc % 4] = mm_o
                        ofc = h // 2
                        ro = 64 * (h % 2)
                        t_ob_prev = p.emit("scalar",
                                    lambda e, ofc=ofc, ro=ro:
                                    e.copy(ob[ro:ro + 64,
                                              ofc * 512:(ofc + 1) * 512],
                                           psum[6][0:64, 0:512]),
                                    deps=[mm_o, p.now("vector")])
                        ob_done[h] = t_ob_prev
                        prev_ob[h] = t_ob_prev
                        if ATTN_SUM:
                            t_add = p.emit("vector", lambda e:
                                    e.tensor_tensor(sumacc[:, :], sumacc[:, :],
                                                    psum[7][0:16, 0:512],
                                                    OP.add),
                                    deps=[sum_mm, t_sac])
                    if not ATTN_SUM:
                        continue
                    t_rcp = p.emit("vector", lambda e:
                                   e.reciprocal(rcpst[:, :], sumacc[:, :]),
                                   deps=[t_add, p.now("scalar")])
                    norm_toks = []
                    for ofc in range(FC):
                        t_rb = p.emit("tensor", lambda e, ofc=ofc:
                                      e.matmul(psum[3][:, 0:512],
                                               sel16[:, ofc * 128:(ofc + 1) * 128],
                                               rcpst[:, :],
                                               start=True, stop=True),
                                      deps=[t_rcp, tok_sel,
                                            norm_toks[-1] if norm_toks
                                            else p.now("vector")])
                        norm_toks.append(p.emit("vector",
                                        lambda e, ofc=ofc:
                                        e.tensor_tensor(
                                            ob[:, ofc * 512:(ofc + 1) * 512],
                                            ob[:, ofc * 512:(ofc + 1) * 512],
                                            psum[3][:, 0:512], OP.mult),
                                        deps=[t_rb, ob_done[2 * ofc],
                                              ob_done[2 * ofc + 1]]))
                    if DEBUG_TAPS:
                        p.dma("gpsimd", dbg["f_ob"][qc], ob[:, :],
                              deps=[norm_toks[-1]])
                        p.dma("gpsimd", dbg["f_sum"][qc], sumacc[:, :],
                              deps=[t_rcp])
                    if ATTN_PART < 3:
                        continue
                    mm_last = None
                    for ofc in range(FC):
                        wz = wst[:, (ofc % 2) * 1024:(ofc % 2) * 1024 + 1024]
                        dW = p.dma("sync", wz, wo_bf[ofc, :, :],
                                   deps=[tok_wo, p.now("tensor")])
                        for fout in range(FC):
                            mm_last = p.emit("tensor",
                                  lambda e, fout=fout, wz=wz, ofc=ofc:
                                  e.matmul(psum[fout][:, 0:512],
                                           wz[:, fout * 128:(fout + 1) * 128],
                                           ob[:, ofc * 512:(ofc + 1) * 512],
                                           start=(ofc == 0),
                                           stop=(ofc == FC - 1)),
                                  deps=[norm_toks[ofc], dW] +
                                       (res_toks[-FC:] + [norm_toks[-1]]
                                        if ofc == 0 else []))
                    for fout in range(FC):
                        xsl = xs[:, fout * T + qc * 512:
                                 fout * T + (qc + 1) * 512]
                        res_toks.append(p.emit("vector",
                                       lambda e, xsl=xsl, fout=fout:
                                       e.tensor_tensor(xsl,
                                                       psum[fout][:, 0:512],
                                                       xsl, OP.add),
                                       deps=[mm_last]))
                p.barrier()
                return [p.now("vector")]

        cur = [x_tok[-1], x_tok[0]]
        cur = emit_ffn(cur)
        if SUBLAYERS >= 2:
            cur = emit_attn(cur)
        if SUBLAYERS >= 3:
            cur = emit_ffn(cur)

        for f in range(FC):
            p.dma("gpsimd", y_out[f * 128:(f + 1) * 128, :],
                  xs[:, f * T:(f + 1) * T], deps=cur)
        if DEBUG_TAPS:
            p.dma("gpsimd", dbg["d_U"][:, :], U_sp[0, 0, :, :], deps=cur)
            p.dma("gpsimd", dbg["d_G"][:, :], G_sp[0, 0, :, :], deps=cur)
            p.dma("gpsimd", dbg["d_G2"][:, :], G2_sp[0, 0, :, :], deps=cur)
            p.dma("gpsimd", dbg["d_O"][:, :], O_sp[0, 0, :, :], deps=cur)
            p.dma("gpsimd", dbg["d_scl"][:, :], scl[:, :], deps=cur)
            p.dma("gpsimd", dbg["d_sx"][:, :], sx4[:, :], deps=cur)
            p.dma("gpsimd", dbg["d_a4"][:, :], a4[:, :], deps=cur)
            p.dma("gpsimd", dbg["d_wup"][:, :], wup_bf[0, :, :], deps=cur)
            p.dma("gpsimd", dbg["d_wdn"][:, :], wdown_bf[0, :, :], deps=cur)
            for nm, srcb in (("f_U", U_sp), ("f_G", G_sp), ("f_G2", G2_sp),
                             ("f_O", O_sp)):
                for icc in range(IC):
                    p.dma("gpsimd", dbg[nm][icc], srcb[icc], deps=cur)
            for icc in range(IC):
                p.dma("gpsimd", dbg["f_wd"][icc], wdown_bf[icc], deps=cur)
            if SUBLAYERS >= 2:
                for hh in range(NH):
                    p.dma("gpsimd", dbg["f_Qp"][hh], Qp_sp[hh], deps=cur)
                    p.dma("gpsimd", dbg["f_Kp"][hh], Kp_sp[hh], deps=cur)
                for tt in range(16):
                    p.dma("gpsimd", dbg["f_Vp"][tt], Vp_sp[tt], deps=cur)

        p.emit("gpsimd", lambda e: e.engine_nop(),
               deps=[("st", p.cnt["st"])])

        blk = es.enter_context(nc.Block())
        p.run_all(blk)
    return nc



def _host_constants():
    inv = 1.0 / ROPE_BASE ** (np.arange(0, HD, 2, dtype=np.float64) / HD)
    t = np.arange(T, dtype=np.float64)
    fr = np.outer(t, inv)
    cos = np.cos(fr).T.astype(np.float32)
    sin = np.sin(fr).T.astype(np.float32)
    cos4 = np.ascontiguousarray(np.tile(cos, (4, 1)).astype(np.float32))
    sincm = np.ascontiguousarray(
        np.concatenate([-sin, sin, -sin, sin], 0).astype(np.float32))
    ident = np.eye(128, dtype=np.float32)
    swap = np.zeros((128, 128), dtype=np.float32)
    for b in range(2):
        for r in range(32):
            swap[b * 64 + r, b * 64 + 32 + r] = 1.0
            swap[b * 64 + 32 + r, b * 64 + r] = 1.0
    hind = np.zeros((128, 2), dtype=np.float32)
    hind[0:64, 0] = 1.0
    hind[64:128, 1] = 1.0
    hindT = np.ascontiguousarray(hind.T)
    cm = np.zeros((128, 4 * 512), dtype=np.float32)
    for dpos in range(4):
        kk = np.arange(128)[:, None] + dpos * 128
        qq = np.arange(512)[None, :]
        cm[:, dpos * 512:(dpos + 1) * 512] = (qq >= kk).astype(np.float32)
    sel4 = np.zeros((4, TC * 128), dtype=np.float32)
    for tc in range(TC):
        sel4[tc, tc * 128:(tc + 1) * 128] = 1.0
    hindsel = np.zeros((128, 128), dtype=np.float32)
    for c in range(8):
        hindsel[0:64, c * 16 + 2 * c] = 1.0
        hindsel[64:128, c * 16 + 2 * c + 1] = 1.0
    hsel = np.zeros((128, NH * 16), dtype=np.float32)
    for h in range(NH):
        hsel[:, h * 16 + h] = 1.0
    sel16 = np.zeros((16, FC * 128), dtype=np.float32)
    for c in range(FC):
        sel16[2 * c, c * 128:c * 128 + 64] = 1.0
        sel16[2 * c + 1, c * 128 + 64:(c + 1) * 128] = 1.0
    return dict(hsel=np.ascontiguousarray(hsel.astype(ml_dtypes.bfloat16)),
                sel4=np.ascontiguousarray(sel4),
                hindsel=np.ascontiguousarray(hindsel.astype(ml_dtypes.bfloat16)),
                sel16=np.ascontiguousarray(sel16),
                cos4=cos4, sinc=sincm, ident=ident,
                swapm=np.ascontiguousarray(swap.astype(np.float32)),
                hind=np.ascontiguousarray(hind.astype(ml_dtypes.bfloat16)),
                hindT=hindT,
                cmask=np.ascontiguousarray(cm.astype(ml_dtypes.bfloat16)))


def kernel(L0_attn_x, L0_attn_Wq, L0_attn_Wk, L0_attn_Wv, L0_attn_Wo,
           L0_attn_q_norm_w, L0_attn_k_norm_w,
           L0_ffn_w_gate, L0_ffn_w_up, L0_ffn_w_down):
    consts = _host_constants()
    qwv = np.ascontiguousarray(
        np.tile(np.asarray(L0_attn_q_norm_w, np.float32), 2)[:, None])
    kwv = np.ascontiguousarray(
        np.tile(np.asarray(L0_attn_k_norm_w, np.float32), 2)[:, None])
    shared = dict(
        Wq=np.ascontiguousarray(np.asarray(L0_attn_Wq, np.float32)),
        Wk=np.ascontiguousarray(np.asarray(L0_attn_Wk, np.float32)),
        Wv=np.ascontiguousarray(np.asarray(L0_attn_Wv, np.float32)),
        Wo=np.ascontiguousarray(np.asarray(L0_attn_Wo, np.float32)),
        Wup=np.ascontiguousarray(np.asarray(L0_ffn_w_up, np.float32)),
        Wgate=np.ascontiguousarray(np.asarray(L0_ffn_w_gate, np.float32)),
        Wdown=np.ascontiguousarray(np.asarray(L0_ffn_w_down, np.float32)),
        qw=qwv, kw=kwv, **consts)
    x = np.asarray(L0_attn_x, np.float32)
    in_maps = []
    for b in range(NCORE):
        m = dict(shared)
        m["x_in"] = np.ascontiguousarray(x[b].T)
        in_maps.append(m)
    nc = build_nc()
    res = run_bass_kernel_spmd(nc, in_maps, core_ids=list(range(NCORE)))
    out = np.empty((NCORE, T, D), np.float32)
    for b in range(NCORE):
        out[b] = np.asarray(res.results[b]["y_out"]).T
    return out
